# version 07
# baseline (speedup 1.0000x reference)
import numpy as np
import ml_dtypes
from contextlib import ExitStack
import concourse.bass as bass
import concourse.mybir as mybir
from concourse.bass_utils import run_bass_kernel_spmd

F32 = mybir.dt.float32
BF16 = mybir.dt.bfloat16
AF = mybir.ActivationFunctionType
ALU = mybir.AluOpType
AX = mybir.AxisListType

H = 2048
S = 4096
B = 2
NHEAD = 16
E = 8
O = 2048
HC = 16
D = 1024
NTB = D // 128
NQT = 2
NKC = S // 128
SCALE = 1.0 / float(np.sqrt(128.0))
CSHIFT = 12.0
TRI_NEG = -1.0e12
KB_NEG = -1.0e10
EPS = 1e-5
BIGM = 120.0
SO = 64
NSPAN = O // SO

bf16 = ml_dtypes.bfloat16


def build_program(debug=False, detect_races=True):
    nc = bass.Bass("TRN2", target_bir_lowering=False, detect_race_conditions=detect_races)

    xt = nc.dram_tensor("xt", [H, S], F32, kind="ExternalInput").ap()
    xt_own = nc.dram_tensor("xt_own", [16, 128, D], F32, kind="ExternalInput").ap()
    wqkv = nc.dram_tensor("wqkv", [NHEAD, 128, 2 * HC * 128], F32, kind="ExternalInput").ap()
    wv = nc.dram_tensor("wv", [NHEAD, 128, HC * 128], F32, kind="ExternalInput").ap()
    v_hbm_t = nc.dram_tensor("v_hbm", [NKC, 128, NHEAD, 128], BF16)
    v_hbm = v_hbm_t.ap()
    wout = nc.dram_tensor("wout", [16, 128, HC * 128], F32, kind="ExternalInput").ap()
    rw = nc.dram_tensor("rw", [NSPAN, 128, HC * E * SO], F32, kind="ExternalInput").ap()
    rb = nc.dram_tensor("rb", [NSPAN, E, SO], F32, kind="ExternalInput").ap()
    ew = nc.dram_tensor("ew", [16, 128, E * HC * 128], F32, kind="ExternalInput").ap()
    eb = nc.dram_tensor("eb", [E, O], F32, kind="ExternalInput").ap()
    tri = nc.dram_tensor("tri", [4, 128, 512], BF16, kind="ExternalInput").ap()
    kbias = nc.dram_tensor("kbias", [128, NKC], F32, kind="ExternalInput").ap()
    ident = nc.dram_tensor("ident", [128, 128], BF16, kind="ExternalInput").ap()
    ue = nc.dram_tensor("ue", [E, E * 128], BF16, kind="ExternalInput").ap()
    out_t = nc.dram_tensor("out_t", [16, 128, D], F32, kind="ExternalOutput").ap()
    attn_hbm = nc.dram_tensor("attn_hbm", [NHEAD, 128, D], BF16).ap()
    xn_hbm = nc.dram_tensor("xn_hbm", [HC, 128, D], BF16).ap()

    dbg = {}
    if debug:
        dbg["xnT"] = nc.dram_tensor("dbg_xnT", [HC, 128, S], BF16, kind="ExternalOutput").ap()
        dbg["attnT"] = nc.dram_tensor("dbg_attnT", [NHEAD, 128, D], BF16, kind="ExternalOutput").ap()
        dbg["c"] = nc.dram_tensor("dbg_c", [NTB, 128, E], F32, kind="ExternalOutput").ap()
        dbg["cparts"] = nc.dram_tensor("dbg_cparts", [128, NTB * 8 * E], F32, kind="ExternalOutput").ap()
        dbg["lgq"] = nc.dram_tensor("dbg_lgq", [128, NTB * E * 256], BF16, kind="ExternalOutput").ap()
        dbg["rwsb"] = nc.dram_tensor("dbg_rwsb", [128, HC * E * SO], BF16, kind="ExternalOutput").ap()

    ENGS = ("sync", "gpsimd", "tensor", "vector", "scalar")
    ops = {e: [] for e in ENGS}
    cnt = {"dS": 0, "dG": 0, "dC": 0, "dEB": 0, "dXN": 0,
           "sX0": 0, "sX1": 0, "sN0": 0, "sN1": 0, "sN2": 0,
           "sW0": 0, "sW1": 0, "sR0": 0, "sR1": 0, "sE0": 0, "sE1": 0,
           "sF0": 0, "sF1": 0, "dOS": 0, "sN3": 0, "sVS0": 0, "sVS1": 0, "sL0": 0, "sL1": 0, "sL2": 0, "sL3": 0,
           "T": 0, "V": 0, "Sc": 0}

    ctx = ExitStack()
    sems = {s: ctx.enter_context(nc.semaphore("sem_" + s)) for s in cnt}

    def emit(eng, fn):
        ops[eng].append(fn)

    def wait(eng, sem, val):
        if val is None or val <= 0:
            return
        emit(eng, lambda e, s=sems[sem], v=val: e.wait_ge(s, v))

    def dma(eng, dst, src, key):
        cnt[key] += 16
        v = cnt[key]
        emit(eng, lambda e, d=dst, s=src, sm=sems[key]: e.dma_start(d, s).then_inc(sm, 16))
        return v

    def t_mm(psum, lhsT, rhs, start, stop, inc=False):
        if inc:
            cnt["T"] += 1
            emit("tensor", lambda e, o=psum, l=lhsT, r=rhs, st=start, sp=stop, sm=sems["T"]:
                 e.matmul(o, l, r, start=st, stop=sp).then_inc(sm, 1))
            return cnt["T"]
        emit("tensor", lambda e, o=psum, l=lhsT, r=rhs, st=start, sp=stop:
             e.matmul(o, l, r, start=st, stop=sp))
        return None

    def t_tr(psum, src, idn, inc=False):
        if inc:
            cnt["T"] += 1
            emit("tensor", lambda e, o=psum, s=src, i=idn, sm=sems["T"]:
                 e.transpose(o, s, i).then_inc(sm, 1))
            return cnt["T"]
        emit("tensor", lambda e, o=psum, s=src, i=idn: e.transpose(o, s, i))
        return None

    def v_op(name, *args, inc=False, **kw):
        if inc:
            cnt["V"] += 1
            emit("vector", lambda e, n=name, a=args, k=kw, sm=sems["V"]:
                 getattr(e, n)(*a, **k).then_inc(sm, 1))
            return cnt["V"]
        emit("vector", lambda e, n=name, a=args, k=kw: getattr(e, n)(*a, **k))
        return None

    def s_op(name, *args, inc=False, **kw):
        if inc:
            cnt["Sc"] += 1
            emit("scalar", lambda e, n=name, a=args, k=kw, sm=sems["Sc"]:
                 getattr(e, n)(*a, **k).then_inc(sm, 1))
            return cnt["Sc"]
        emit("scalar", lambda e, n=name, a=args, k=kw: getattr(e, n)(*a, **k))
        return None

    SB = lambda st, name, shape, dt=F32, **kw: st.enter_context(nc.sbuf_tensor(name, shape, dt, **kw))
    PS = lambda st, name, shape, dt=F32: st.enter_context(nc.psum_tensor(name, shape, dt))

    ident_sb = SB(ctx, "ident_sb", [128, 128], BF16, side="right")
    tri_sb = SB(ctx, "tri_sb", [128, 4 * 512], BF16, side="right")
    kb_sb = SB(ctx, "kb_sb", [128, NKC], F32, side="right")
    ones1 = SB(ctx, "ones1", [1, 128], BF16, side="right")
    ones1f = SB(ctx, "ones1f", [1, 128], F32, side="right")
    eps_sb = SB(ctx, "eps_sb", [1, 1], F32, side="right")
    neg30_pp = SB(ctx, "neg30_pp", [128, 1], F32, side="right")
    onesc = SB(ctx, "onesc", [128, 1], BF16, side="right")
    stg2k = [SB(ctx, "stg2k%d" % i, [128, 2048], F32, side="right") for i in range(2)]

    dma("sync", ident_sb[:, :], ident, "dC")
    dma("sync", tri_sb[:, :].rearrange("p (n q) -> p n q", n=4), tri.rearrange("n p q -> p n q"), "dC")
    dma("sync", kb_sb[:, :], kbias, "dC")
    setup_dC = cnt["dC"]
    v_op("memset", ones1[:, :], 1.0)
    v_op("memset", ones1f[:, :], 1.0)
    v_op("memset", eps_sb[:, :], EPS)
    v_op("memset", neg30_pp[:, :], -30.0)
    v_ones = v_op("memset", onesc[:, :], 1.0, inc=True)

    xnT_st = ExitStack()
    xnT = SB(xnT_st, "xnT", [128, HC * S], BF16, side="left")

    ln_st = ExitStack()
    xstgf = [SB(ln_st, "xstgf%d" % i, [128, 1024]) for i in range(4)]
    xstrip = [SB(ln_st, "xstrip%d" % i, [128, 1024], BF16) for i in range(2)]
    sqstrip = [SB(ln_st, "sqstrip%d" % i, [128, 1024], BF16) for i in range(2)]
    nstr = [SB(ln_st, "nstr%d" % i, [128, 512]) for i in range(2)]
    tscr = [SB(ln_st, "tscr%d" % i, [128, 512]) for i in range(1)]
    m_bc = [SB(ln_st, "m_bc%d" % i, [128, 512]) for i in range(2)]
    r_bc = [SB(ln_st, "r_bc%d" % i, [128, 512]) for i in range(2)]
    mrow = SB(ln_st, "mrow", [1, 1024], F32)
    sqrow = SB(ln_st, "sqrow", [1, 1024], F32)
    r_mn = SB(ln_st, "r_mn", [1, 512], F32)
    r_msq = SB(ln_st, "r_msq", [1, 512], F32)
    r_var = SB(ln_st, "r_var", [1, 512], F32)
    r_vb = SB(ln_st, "r_vb", [1, 512], F32)
    r_rstd = SB(ln_st, "r_rstd", [1, 512], F32)
    onescf = SB(ln_st, "onescf", [128, 1], F32)
    ln_ps = ExitStack()
    p_stat = [PS(ln_ps, "p_stat%d" % i, [1, 512]) for i in range(4)]
    p_bc0 = PS(ln_ps, "p_bc0", [128, 512])
    p_bc1 = PS(ln_ps, "p_bc1", [128, 512])

    v_onesf = v_op("memset", onescf[:, :], 1.0, inc=True)
    strip_T = {}
    strip_V = {}
    norm_V = {}
    norm_vn = {}
    stat_evac_V = 0
    for qr in range(4):
        for hc in range(HC):
            li = qr * HC + hc
            ib = li % 4
            sq = li % 2
            skey = "sL%d" % ib
            wait("sync", "V", strip_V.get(li - 4))
            tok = dma("sync", xstgf[ib][:, :],
                      xt[hc * 128:(hc + 1) * 128, qr * 1024:(qr + 1) * 1024], skey)
            wait("vector", "T", strip_T.get(li - 2))
            wait("vector", skey, tok)
            v_op("tensor_copy", xstrip[sq][:, :], xstgf[ib][:, :])
            vsq = v_op("tensor_mul", sqstrip[sq][:, :], xstrip[sq][:, :], xstrip[sq][:, :], inc=True)
            strip_V[li] = vsq
            wait("tensor", "V", vsq)
            if li == 0:
                wait("tensor", "V", v_ones)
            if hc == 0:
                wait("tensor", "V", stat_evac_V)
            for tt in range(2):
                t_mm(p_stat[tt][:, :], onesc[:, :], xstrip[sq][:, tt * 512:(tt + 1) * 512],
                     start=(hc == 0), stop=(hc == HC - 1))
                t_mm(p_stat[2 + tt][:, :], onesc[:, :], sqstrip[sq][:, tt * 512:(tt + 1) * 512],
                     start=(hc == 0), stop=(hc == HC - 1), inc=(tt == 1))
            strip_T[li] = cnt["T"]
        wait("vector", "T", cnt["T"])
        for tt in range(2):
            v_op("tensor_copy", mrow[0:1, tt * 512:(tt + 1) * 512], p_stat[tt][:, :])
            v_op("tensor_copy", sqrow[0:1, tt * 512:(tt + 1) * 512], p_stat[2 + tt][:, :],
                 inc=(tt == 1))
        stat_evac_V = cnt["V"]

        for ttq in range(2):
            tt = qr * 2 + ttq
            off = ttq * 512
            bb = tt % 2
            wait("vector", "Sc", norm_V.get(("rs", tt - 1)))
            v_op("tensor_scalar_mul", r_mn[0:1, :], mrow[0:1, off:off + 512], 1.0 / 2048.0)
            v_op("tensor_mul", r_msq[0:1, :], r_mn[0:1, :], r_mn[0:1, :])
            v_op("tensor_scalar_mul", r_var[0:1, :], sqrow[0:1, off:off + 512], 1.0 / 2048.0)
            v_var = v_op("tensor_sub", r_vb[0:1, :], r_var[0:1, :], r_msq[0:1, :], inc=True)
            wait("scalar", "V", v_var)
            s_op("activation", r_msq[0:1, :], r_vb[0:1, :], AF.Ln, bias=eps_sb[0:1, :])
            s_rstd = s_op("activation", r_rstd[0:1, :], r_msq[0:1, :], AF.Exp, scale=-0.5, inc=True)
            norm_V[("rs", tt)] = s_rstd
            wait("tensor", "Sc", s_rstd)
            wait("tensor", "V", norm_V.get(tt - 1))
            t_mm(p_bc0[:, :], ones1f[:, :], r_mn[0:1, :], start=True, stop=True)
            tv = t_mm(p_bc1[:, :], ones1f[:, :], r_rstd[0:1, :], start=True, stop=True, inc=True)
            wait("vector", "T", tv)
            v_op("tensor_copy", m_bc[bb][:, :], p_bc0[:, :])
            norm_V[tt] = v_op("tensor_copy", r_bc[bb][:, :], p_bc1[:, :], inc=True)
            for hc in range(HC):
                nidx = tt * HC + hc
                ib = nidx % 2
                sc = nidx % 2
                nkey = "sN%d" % (1 + ib)
                wait("sync", "V", norm_vn.get(nidx - 2))
                tok = dma("sync", nstr[ib][:, :],
                          xt[hc * 128:(hc + 1) * 128, tt * 512:(tt + 1) * 512], nkey)
                wait("vector", nkey, tok)
                v_op("tensor_sub", tscr[0][:, :], nstr[ib][:, :], m_bc[bb][:, :])
                vn = v_op("tensor_mul", xnT[:, hc * S + tt * 512: hc * S + (tt + 1) * 512],
                          tscr[0][:, :], r_bc[bb][:, :], inc=True)
                norm_vn[nidx] = vn
                if hc == HC - 1:
                    norm_V[tt] = vn
    ln_done_V = cnt["V"]
    ln_ps.close()

    wait("sync", "V", ln_done_V)
    xn_hbm_tok = 0
    for hc in range(HC):
        xn_hbm_tok = dma("sync", xn_hbm[hc], xnT[:, hc * S + 3072: hc * S + 4096], "dS")
    if debug:
        for hc in range(HC):
            dma("sync", dbg["xnT"][hc], xnT[:, hc * S:(hc + 1) * S], "dS")
    ln_st.close()

    vp_st = ExitStack()
    wv_half = SB(vp_st, "wv_half", [128, HC * 8 * 128], BF16)
    vstrip = [SB(vp_st, "vstrip%d" % i, [128, 512], BF16) for i in range(2)]
    vp_ps = ExitStack()
    p_vp = [PS(vp_ps, "p_vp%d" % i, [128, 512]) for i in range(2)]

    wait("tensor", "dC", setup_dC)
    wait("scalar", "dC", setup_dC)
    wait("vector", "dC", setup_dC)

    vpre_half_dS = {}
    vp_ev = {}
    vp_store = {}
    vp_conv = {}
    vgi = 0
    for half in range(2):
        cvl = None
        for h8 in range(8):
            head = half * 8 + h8
            fkey = "sF%d" % (h8 % 2)
            wait("sync", "V", vp_conv.get((half, h8 - 2)))
            if half > 0 and h8 < 2:
                wait("sync", "V", vp_conv.get((half - 1, 6 + h8)))
            ftok = dma("sync", stg2k[h8 % 2][:, :], wv[head], fkey)
            wait("vector", fkey, ftok)
            if half > 0 and h8 == 0:
                wait("vector", "T", cnt["T"])
            if head == 0:
                wait("vector", "V", ln_done_V)
            cvl = v_op("tensor_copy",
                       wv_half[:, :].rearrange("p (hc h c) -> p hc h c", h=8, c=128)[:, :, h8, :],
                       stg2k[h8 % 2][:, :].rearrange("p (hc c) -> p hc c", c=128), inc=True)
            vp_conv[(half, h8)] = cvl
        wait("tensor", "V", cvl)
        if half == 0:
            wait("tensor", "V", ln_done_V)
        for tc in range(NKC):
            for grp in range(2):
                gi = vgi
                vgi += 1
                pv = p_vp[gi % 2]
                wait("tensor", "Sc", vp_ev.get(gi - 2))
                for hc in range(HC):
                    t_mm(pv[:, :], xnT[:, hc * S + tc * 128:(hc + 1 - 1) * S + hc * S + (tc + 1) * 128]
                         if False else xnT[:, hc * S + tc * 128: hc * S + (tc + 1) * 128],
                         wv_half[:, (hc * 8 + grp * 4) * 128:(hc * 8 + (grp + 1) * 4) * 128],
                         start=(hc == 0), stop=(hc == HC - 1), inc=(hc == HC - 1))
                wait("scalar", "T", cnt["T"])
                wait("scalar", "sVS%d" % (gi % 2), vp_store.get(gi - 2))
                ev = s_op("copy", vstrip[gi % 2][:, :], pv[:, :], inc=True)
                vp_ev[gi] = ev
                wait("sync", "Sc", ev)
                vp_store[gi] = dma("sync",
                                   v_hbm[tc][:, half * 8 + grp * 4: half * 8 + (grp + 1) * 4, :],
                                   vstrip[gi % 2][:, :].rearrange("p (h d) -> p h d", d=128),
                                   "sVS%d" % (gi % 2))
        vpre_half_dS[half] = (cnt["sVS0"], cnt["sVS1"])
    vp_ps.close()
    vp_st.close()

    at_st = ExitStack()
    NWC = 2 * HC
    wg_sb = [SB(at_st, "wg_sb%d" % i, [128, NWC * 128], BF16) for i in range(2)]
    ktbuf = SB(at_st, "ktbuf", [128, S], BF16)
    qtbuf = SB(at_st, "qtbuf", [128, D], BF16)
    vaug = [SB(at_st, "vaug%d" % i, [128, NKC * 129], BF16) for i in range(2)]
    expt = [SB(at_st, "expt%d" % i, [128, 512], BF16) for i in range(4)]
    mskscr = [SB(at_st, "mskscr%d" % i, [128, 512]) for i in range(2)]
    aosb = [SB(at_st, "aosb%d" % i, [128, 128], BF16) for i in range(2)]
    rec_sb = SB(at_st, "rec_sb", [128, 8], F32)
    astrip = SB(at_st, "astrip", [128, D], BF16)
    at_ps = ExitStack()
    p_stage = PS(at_ps, "p_stage", [128, 512])
    p_sc = [PS(at_ps, "p_sc%d" % i, [128, 512]) for i in range(2)]
    p_av = [PS(at_ps, "p_av%d" % i, [128, 129]) for i in range(4)]
    p_tr = PS(at_ps, "p_tr", [128, 128], BF16)
    stage_ps = [p_stage, p_sc[0]]

    def wgc(buf, role, hc):
        i = role * HC + hc
        return wg_sb[buf][:, i * 128:(i + 1) * 128]

    head_T = {}
    ev_Sc = {}
    stage_ev = [None, None]
    stage_i = 0
    av_T = {}
    tr_T = {}
    store_tok = {}
    wg_conv = {}
    va_tok = {}

    for head in range(NHEAD):
        gb = head % 2
        cvw = None
        for sl in range(2):
            fkey = "sF%d" % (sl % 2)
            if head == 0:
                wait("sync", "V", vp_conv.get((1, 6 + sl)))
            wait("sync", "V", wg_conv.get((head - 1, sl)))
            ftok = dma("sync", stg2k[sl][:, :], wqkv[head][:, sl * 2048:(sl + 1) * 2048], fkey)
            wait("vector", fkey, ftok)
            wait("vector", "T", head_T.get(head - 2))
            cvw = v_op("tensor_copy", wg_sb[gb][:, sl * 2048:(sl + 1) * 2048],
                       stg2k[sl][:, :], inc=True)
            wg_conv[(head, sl)] = cvw
        wait("sync", "sVS0", vpre_half_dS[head // 8][0])
        wait("sync", "sVS1", vpre_half_dS[head // 8][1])
        wait("sync", "T", head_T.get(head - 2))
        va_tok[head] = dma("sync", vaug[gb][:, :].rearrange("p (k n) -> p k n", n=129)[:, :, 0:128],
                           bass.AP(v_hbm_t, head * 128,
                                   [[NHEAD * 128, 128], [128 * NHEAD * 128, NKC], [1, 128]]),
                           "sW%d" % gb)
        wait("vector", "sW%d" % gb, va_tok[head])
        v_onesc = v_op("memset", vaug[gb][:, :].rearrange("p (k n) -> p k n", n=129)[:, :, 128:129],
                       1.0, inc=True)
        wait("tensor", "V", cvw)
        if head == 0:
            wait("tensor", "V", ln_done_V)
        for kt in range(8):
            sbk = stage_ps[stage_i % 2]
            wait("tensor", "Sc", stage_ev[stage_i % 2])
            for hc in range(HC):
                t_mm(sbk[:, :], wgc(gb, 0, hc), xnT[:, hc * S + kt * 512: hc * S + (kt + 1) * 512],
                     start=(hc == 0), stop=(hc == HC - 1), inc=(hc == HC - 1))
            wait("scalar", "T", cnt["T"])
            last_stage_ev = s_op("copy", ktbuf[:, kt * 512:(kt + 1) * 512], sbk[:, :], inc=True)
            stage_ev[stage_i % 2] = last_stage_ev
            stage_i += 1
        for qt in range(NQT):
            sbk = stage_ps[stage_i % 2]
            wait("tensor", "Sc", stage_ev[stage_i % 2])
            for hc in range(HC):
                t_mm(sbk[:, :], wgc(gb, 1, hc),
                     xnT[:, hc * S + 3072 + qt * 512: hc * S + 3072 + (qt + 1) * 512],
                     start=(hc == 0), stop=(hc == HC - 1), inc=(hc == HC - 1))
            wait("scalar", "T", cnt["T"])
            last_stage_ev = s_op("copy", qtbuf[:, qt * 512:(qt + 1) * 512], sbk[:, :], inc=True)
            stage_ev[stage_i % 2] = last_stage_ev
            stage_i += 1
        wait("tensor", "Sc", cnt["Sc"])
        wait("tensor", "V", cnt["V"])

        for qt in range(NQT):
            nkc = 28 + 4 * qt
            diag0 = 24 + 4 * qt

            sc_ps = [p_sc[0], p_sc[1], p_stage]

            def emit_score(kc):
                pb = kc % 3
                wait("tensor", "Sc", ev_Sc.get(("exp", head, qt, kc - 3)))
                tv = t_mm(sc_ps[pb][:, :], ktbuf[:, kc * 128:(kc + 1) * 128],
                          qtbuf[:, qt * 512:(qt + 1) * 512], start=True, stop=True, inc=True)
                dd = kc - diag0
                if dd >= 0:
                    wait("vector", "T", tv)
                    wait("vector", "Sc", ev_Sc.get(("exp", head, qt, kc - 2)))
                    vmask = v_op("tensor_add", mskscr[kc % 2][:, :], sc_ps[pb][:, :],
                                 tri_sb[:, dd * 512:(dd + 1) * 512], inc=True)
                    wait("scalar", "V", vmask)
                    src = mskscr[kc % 2][:, :]
                else:
                    wait("scalar", "T", tv)
                    src = sc_ps[pb][:, :]
                eb_ = kc % 4
                wait("scalar", "T", av_T.get((head, qt, kc - 4)))
                ex = s_op("activation", expt[eb_][:, :], src, AF.Exp,
                          bias=kb_sb[:, kc:kc + 1], scale=SCALE, inc=True)
                ev_Sc[("exp", head, qt, kc)] = ex

            def emit_av(kc):
                eb_ = kc % 4
                wait("tensor", "Sc", ev_Sc[("exp", head, qt, kc)])
                for qs in range(4):
                    t_mm(p_av[qs][:, :], expt[eb_][:, qs * 128:(qs + 1) * 128],
                         vaug[gb][:, kc * 129:(kc + 1) * 129],
                         start=(kc == 0), stop=(kc == nkc - 1), inc=(qs == 3))
                av_T[(head, qt, kc)] = cnt["T"]

            emit_score(0)
            emit_score(1)
            for kc in range(2, nkc):
                emit_score(kc)
                emit_av(kc - 2)
            emit_av(nkc - 2)
            emit_av(nkc - 1)
            wait("vector", "T", cnt["T"])
            for qs in range(4):
                v_op("reciprocal", rec_sb[:, qs:qs + 1], p_av[qs][:, 128:129],
                     inc=(qs == 3))
            wait("scalar", "V", cnt["V"])
            for qs in range(4):
                ob = qs % 2
                wait("scalar", "T", tr_T.get((head, qt, qs - 2)))
                sa = s_op("activation", aosb[ob][:, :], p_av[qs][:, 0:128], AF.Copy,
                          scale=rec_sb[:, qs:qs + 1], inc=True)
                wait("tensor", "Sc", sa)
                ttr = t_tr(p_tr[:, :], aosb[ob][:, :], ident_sb[:, :], inc=True)
                tr_T[(head, qt, qs)] = ttr
                wait("vector", "T", ttr)
                vc = v_op("tensor_copy", astrip[:, qt * 512 + qs * 128: qt * 512 + (qs + 1) * 128],
                          p_tr[:, :], inc=True)
                wait("tensor", "V", vc)
        stage_ev[1] = max(stage_ev[1] or 0, ev_Sc.get(("exp", head, NQT - 1, 30), 0),
                          ev_Sc.get(("exp", head, NQT - 1, 31), 0))
        stage_ev[0] = max(stage_ev[0] or 0, ev_Sc.get(("exp", head, NQT - 1, 29), 0),
                          ev_Sc.get(("exp", head, NQT - 1, 31), 0))
        wait("sync", "V", cnt["V"])
        store_tok[head] = dma("sync", attn_hbm[head], astrip[:, :], "dS")
        wait("vector", "dS", store_tok[head])
        head_T[head] = cnt["T"]

    att_T_done = cnt["T"]
    att_store_dS = cnt["dS"]
    at_ps.close()
    at_st.close()
    xnT_st.close()

    rt_keep = ExitStack()
    xnT_own = SB(rt_keep, "xnT_own", [128, HC * D], BF16)
    cbc = SB(rt_keep, "cbc", [128, E * D], BF16)
    cT8 = SB(rt_keep, "cT8", [E, D], BF16)
    c_fin = SB(rt_keep, "c_fin", [128, NTB * E], F32)
    cparts = SB(rt_keep, "cparts", [128, NTB * 8 * E], F32)
    ue_sb = SB(rt_keep, "ue_sb", [E, E * 128], BF16)
    rt_st = ExitStack()
    rw_sb = [SB(rt_st, "rw_sb%d" % i, [128, HC * E * SO], BF16) for i in range(2)]
    rb_sb = [SB(rt_st, "rb_sb%d" % i, [1, E * SO], BF16) for i in range(2)]
    lgq = [SB(rt_st, "lgq%d" % i, [128, NTB * E * 256], BF16) for i in range(2)]
    m1s = SB(rt_st, "m1s", [128, 7 * 256], BF16)
    m2s = SB(rt_st, "m2s", [128, 7 * 256], BF16)
    geA = SB(rt_st, "geA", [128, E * 256], BF16)
    lms = SB(rt_st, "lms", [128, E * 256], BF16)
    p12s = SB(rt_st, "p12s", [128, 256], BF16)
    s1s = SB(rt_st, "s1s", [128, E * 256], BF16)
    sigs = SB(rt_st, "sigs", [128, E * 256], BF16)
    ctree = SB(rt_st, "ctree", [128, 6 * NTB * E], F32)
    cbf = SB(rt_st, "cbf", [128, NTB * E], BF16)
    rt_ps = ExitStack()
    p_lg = [PS(rt_ps, "p_lg%d" % i, [128, 512]) for i in range(4)]
    p_ct = PS(rt_ps, "p_ct", [E, 128], BF16)
    p_cb = [PS(rt_ps, "p_cb%d" % i, [128, 512]) for i in range(2)]

    wait("gpsimd", "T", att_T_done)
    wait("gpsimd", "dS", att_store_dS)
    wait("sync", "T", att_T_done)
    wait("sync", "dS", att_store_dS)
    d_ue = dma("sync", ue_sb[:, :], ue, "dC")
    xo_tok = 0
    for hc in range(HC):
        xo_tok = dma("gpsimd", xnT_own[:, hc * D:(hc + 1) * D], xn_hbm[hc], "dXN")

    rw_T = {}
    rstg_hist = []
    lg_Sc = {}
    lgq_done_Sc = {}
    quad_V = {}
    last_sig = [0]
    ge2_done = {}

    def emit_routing_tb(q, tb):
        qb = q % 2
        if tb == 0:
            wait("vector", "Sc", lgq_done_Sc[q])
        wait("vector", "Sc", last_sig[0])
        if True:
            L = lgq[qb][:, tb * E * 256:(tb + 1) * E * 256].rearrange("p (e o) -> p e o", o=256)
            M1 = m1s[:, :].rearrange("p (e o) -> p e o", o=256)
            M2 = m2s[:, :].rearrange("p (e o) -> p e o", o=256)
            v_op("tensor_max", M1[:, 0:4], L[:, 0:4], L[:, 4:8])
            v_op("tensor_max", M1[:, 4:6], M1[:, 0:2], M1[:, 2:4])
            v_op("tensor_max", M1[:, 6:7], M1[:, 4:5], M1[:, 5:6])
            GE = geA[:, :].rearrange("p (e o) -> p e o", o=256)
            v_op("tensor_tensor", GE, L, M1[:, 6:7].to_broadcast([128, E, 256]), op=ALU.is_ge)
            v_op("tensor_scalar_mul", lms[:, :], geA[:, :], BIGM)
            LM = lms[:, :].rearrange("p (e o) -> p e o", o=256)
            v_op("tensor_sub", LM, L, LM)
            v_op("tensor_max", M2[:, 0:4], LM[:, 0:4], LM[:, 4:8])
            v_op("tensor_max", M2[:, 4:6], M2[:, 0:2], M2[:, 2:4])
            v_op("tensor_max", M2[:, 6:7], M2[:, 4:5], M2[:, 5:6])
            v_op("tensor_add", p12s[:, :], M1[:, 6:7].squeeze(1), M2[:, 6:7].squeeze(1))
            v_op("tensor_scalar_mul", s1s[:, :], lgq[qb][:, tb * E * 256:(tb + 1) * E * 256], 2.0)
            S1 = s1s[:, :].rearrange("p (e o) -> p e o", o=256)
            v_op("tensor_sub", S1, S1, p12s[:, :].unsqueeze(1).to_broadcast([128, E, 256]))
            v_op("tensor_tensor", GE, L, M2[:, 6:7].to_broadcast([128, E, 256]), op=ALU.is_ge)
            v_op("tensor_scalar_mul", lms[:, :], geA[:, :], 30.0)
            vs1 = v_op("tensor_add", s1s[:, :], s1s[:, :], lms[:, :], inc=True)
            wait("scalar", "V", vs1)
            ssig = None
            for e_ in range(E):
                ssig = s_op("activation", sigs[:, e_ * 256:(e_ + 1) * 256],
                            s1s[:, e_ * 256:(e_ + 1) * 256], AF.Sigmoid,
                            bias=neg30[0:1, :] if False else neg30_pp[:, 0:1],
                            accum_out=cparts[:, (tb * 8 + q) * E + e_:(tb * 8 + q) * E + e_ + 1],
                            inc=(e_ == E - 1))
            quad_V[q] = cnt["Sc"]
            last_sig[0] = cnt["Sc"]

    for q in range(8):
        for s4 in range(4):
            sp = q * 4 + s4
            rbuf = sp % 2
            rcv = None
            BISECT_CAST = False
            if BISECT_CAST:
                rkeyw = "sR%d" % rbuf
                wait("gpsimd", "T", rw_T.get(sp - 2))
                dma("gpsimd", rw_sb[rbuf][:, :], rw[sp], rkeyw)
            else:
                for sl in range(4):
                    fkey = "sF%d" % (sl % 2)
                    if len(rstg_hist) >= 2:
                        wait("sync", "Sc", rstg_hist[-2])
                    ftok = dma("sync", stg2k[sl % 2][:, :],
                               rw[sp][:, sl * 2048:(sl + 1) * 2048], fkey)
                    wait("scalar", fkey, ftok)
                    wait("scalar", "T", rw_T.get(sp - 2))
                    rcv = s_op("copy", rw_sb[rbuf][:, sl * 2048:(sl + 1) * 2048],
                               stg2k[sl % 2][:, :], inc=True)
                    rstg_hist.append(rcv)
            rkey = "sR%d" % rbuf
            wait("gpsimd", "T", rw_T.get(sp - 2))
            tokb = dma("gpsimd", rb_sb[rbuf][:, :].rearrange("p (e o) -> p e o", o=SO),
                       rb[sp].unsqueeze(0), rkey)
            wait("tensor", rkey, tokb)
            if rcv is not None:
                wait("tensor", "Sc", rcv)
            else:
                wait("tensor", "sR%d" % rbuf, cnt["sR%d" % rbuf])
            if sp == 0:
                wait("tensor", "dXN", xo_tok)
            for tb in range(NTB):
                li = sp * NTB + tb
                pbank = p_lg[li % 4]
                wait("tensor", "Sc", lg_Sc.get(li - 4))
                if s4 == 0 and tb == 0 and (q - 2) in quad_V:
                    wait("scalar", "Sc", quad_V[q - 2])
                for hc in range(HC):
                    t_mm(pbank[:, :], xnT_own[:, hc * D + tb * 128: hc * D + (tb + 1) * 128],
                         rw_sb[rbuf][:, hc * E * SO:(hc + 1) * E * SO],
                         start=(hc == 0), stop=False)
                t_mm(pbank[:, :], ones1[:, :], rb_sb[rbuf][:, :], start=False, stop=True, inc=True)
                wait("scalar", "T", cnt["T"])
                lg_Sc[li] = s_op(
                    "copy",
                    lgq[q % 2][:, tb * E * 256:(tb + 1) * E * 256]
                    .rearrange("p (e o) -> p e o", o=256)[:, :, s4 * SO:(s4 + 1) * SO],
                    pbank[:, :].rearrange("p (e o) -> p e o", o=SO), inc=True)
            rw_T[sp] = cnt["T"]
            if q > 0:
                emit_routing_tb(q - 1, 2 * s4)
                emit_routing_tb(q - 1, 2 * s4 + 1)
        lgq_done_Sc[q] = cnt["Sc"]
    for tb in range(NTB):
        emit_routing_tb(7, tb)

    wait("vector", "Sc", quad_V[7])
    CPA = cparts[:, :].rearrange("p (t q e) -> p t q e", q=8, e=E)
    CTA = ctree[:, 0:4 * NTB * E].rearrange("p (t q e) -> p t q e", q=4, e=E)
    CTB = ctree[:, 4 * NTB * E:6 * NTB * E].rearrange("p (t q e) -> p t q e", q=2, e=E)
    v1 = v_op("tensor_add", CTA, CPA[:, :, 0:4], CPA[:, :, 4:8], inc=True)
    wait("vector", "V", v1)
    v2 = v_op("tensor_add", CTB, CTA[:, :, 0:2], CTA[:, :, 2:4], inc=True)
    wait("vector", "V", v2)
    v3 = v_op("tensor_add", c_fin[:, :].rearrange("p (t e) -> p t e", e=E),
              CTB[:, :, 0, :], CTB[:, :, 1, :], inc=True)
    wait("vector", "V", v3)
    c_done_V = cnt["V"]

    if debug:
        wait("sync", "V", c_done_V)
        for tb in range(NTB):
            dma("sync", dbg["c"][tb], c_fin[:, tb * E:(tb + 1) * E], "dS")
        dma("sync", dbg["cparts"], cparts[:, :], "dS")
        dma("sync", dbg["lgq"], lgq[1][:, :], "dS")
        dma("sync", dbg["rwsb"], rw_sb[1][:, :], "dS")

    vcb = v_op("tensor_copy", cbf[:, :], c_fin[:, :], inc=True)
    wait("tensor", "V", vcb)
    for tb in range(NTB):
        ttr = t_tr(p_ct[:, :], cbf[:, tb * E:(tb + 1) * E], ident_sb[:, :], inc=True)
        wait("scalar", "T", ttr)
        vv = s_op("copy", cT8[:, tb * 128:(tb + 1) * 128], p_ct[:, :], inc=True)
        wait("tensor", "Sc", vv)
    wait("tensor", "dC", d_ue)
    for e_ in range(E):
        for tt in range(2):
            pcb = p_cb[(e_ * 2 + tt) % 2]
            tv = t_mm(pcb[:, :], ue_sb[:, e_ * 128:(e_ + 1) * 128],
                      cT8[:, tt * 512:(tt + 1) * 512], start=True, stop=True, inc=True)
            wait("scalar", "T", tv)
            vv = s_op("copy", cbc[:, e_ * D + tt * 512: e_ * D + (tt + 1) * 512],
                      pcb[:, :], inc=True)
            wait("tensor", "Sc", vv)
    cbc_done_V = None
    cbc_done_Sc = cnt["Sc"]
    rstg_last_Sc = cnt["Sc"]
    rt_ps.close()
    rt_st.close()

    ex_st = ExitStack()
    ebbf = SB(ex_st, "ebbf", [E, O], BF16)
    attnT = SB(ex_st, "attnT", [128, HC * D], BF16)
    ew_sb = [SB(ex_st, "ew_sb%d" % i, [128, E * HC * 128], BF16) for i in range(2)]
    wo_sb = [SB(ex_st, "wo_sb%d" % i, [128, HC * 128], BF16) for i in range(2)]
    xo_sb = [SB(ex_st, "xo_sb%d" % i, [128, D]) for i in range(2)]
    fin = [SB(ex_st, "fin%d" % i, [128, 512]) for i in range(4)]
    scr = [SB(ex_st, "scr%d" % i, [128, 512]) for i in range(2)]
    ex_ps = ExitStack()
    p_A = [PS(ex_ps, "p_A%d" % i, [128, 512]) for i in range(2)]
    p_B = [PS(ex_ps, "p_B%d" % i, [128, 512]) for i in range(2)]

    wait("sync", "V", c_done_V)
    wait("sync", "Sc", rstg_last_Sc)
    wait("sync", "dS", att_store_dS)
    atok = 0
    for hc in range(HC):
        atok = dma("sync", attnT[:, hc * D:(hc + 1) * D],
                   attn_hbm[hc], "dS")
    if debug:
        wait("sync", "dS", atok)
        for hh in range(NHEAD):
            dma("sync", dbg["attnT"][hh], attn_hbm[hh], "dS")

    wait("gpsimd", "Sc", cbc_done_Sc)
    d_eb = dma("gpsimd", ebbf[:, :], eb, "dEB")
    wait("tensor", "dS", atok)
    wait("tensor", "Sc", cbc_done_Sc)
    wait("tensor", "dEB", d_eb)

    ob_T = {}
    estg_hist = []
    ld_tok = {}
    pA_V = []
    pB_V = []
    fin_dS = {}

    def emit_exp_loads(ob):
        obuf = ob % 2
        ecv = None
        for sl in range(9):
            fkey = "sF%d" % (sl % 2)
            if len(estg_hist) >= 2:
                wait("sync", "Sc", estg_hist[-2])
            src = ew[ob][:, sl * 2048:(sl + 1) * 2048] if sl < 8 else wout[ob]
            ftok = dma("sync", stg2k[sl % 2][:, :], src, fkey)
            wait("scalar", fkey, ftok)
            wait("scalar", "T", ob_T.get(ob - 2))
            dst = (ew_sb[obuf][:, sl * 2048:(sl + 1) * 2048] if sl < 8
                   else wo_sb[obuf][:, :])
            ecv = s_op("copy", dst, stg2k[sl % 2][:, :], inc=True)
            estg_hist.append(ecv)
        ekey = "sE%d" % obuf
        tok_x = dma("sync", xo_sb[obuf][:, :], xt_own[ob], ekey)
        ld_tok[ob] = (tok_x, ecv)

    emit_exp_loads(0)
    for ob in range(16):
        obuf = ob % 2
        if ob + 1 < 16:
            emit_exp_loads(ob + 1)
        tok_x, ecv = ld_tok[ob]
        wait("tensor", "sE%d" % obuf, tok_x)
        wait("tensor", "Sc", ecv)
        for tt in range(2):
            idx = ob * 2 + tt
            if len(pA_V) >= 2:
                wait("tensor", "V", pA_V[-2])
            pa = p_A[idx % 2]
            t_mm(pa[:, :], ebbf[:, ob * 128:(ob + 1) * 128], cT8[:, tt * 512:(tt + 1) * 512],
                 start=True, stop=False)
            for hc in range(HC):
                t_mm(pa[:, :], wo_sb[obuf][:, hc * 128:(hc + 1) * 128],
                     attnT[:, hc * D + tt * 512: hc * D + (tt + 1) * 512],
                     start=False, stop=(hc == HC - 1), inc=(hc == HC - 1))
            wait("vector", "T", cnt["T"])
            wait("vector", "sE%d" % obuf, tok_x)
            wait("vector", "dOS", fin_dS.get(idx - 1))
            va = v_op("tensor_add", fin[0][:, :], pa[:, :], xo_sb[obuf][:, tt * 512:(tt + 1) * 512],
                      inc=True)
            pA_V.append(va)
            for e_ in range(E):
                if len(pB_V) >= 2:
                    wait("tensor", "V", pB_V[-2])
                pb = p_B[e_ % 2]
                for hc in range(HC):
                    t_mm(pb[:, :], ew_sb[obuf][:, (e_ * HC + hc) * 128:(e_ * HC + hc + 1) * 128],
                         xnT_own[:, hc * D + tt * 512: hc * D + (tt + 1) * 512],
                         start=(hc == 0), stop=(hc == HC - 1), inc=(hc == HC - 1))
                wait("vector", "T", cnt["T"])
                v_op("tensor_mul", scr[e_ % 2][:, :], pb[:, :],
                     cbc[:, e_ * D + tt * 512: e_ * D + (tt + 1) * 512])
                vb = v_op("tensor_add", fin[(e_ + 1) % 4][:, :], fin[e_ % 4][:, :],
                          scr[e_ % 2][:, :], inc=True)
                pB_V.append(vb)
            wait("gpsimd", "V", cnt["V"])
            fin_dS[idx] = dma("gpsimd", out_t[ob][:, tt * 512:(tt + 1) * 512],
                              fin[0][:, :], "dOS")
        ob_T[ob] = cnt["T"]

    wait("sync", "dS", cnt["dS"])
    wait("sync", "dOS", cnt["dOS"])
    ex_ps.close()
    ex_st.close()
    rt_keep.close()

    with nc.Block() as block:
        @block.sync
        def _(eng):
            for fn in ops["sync"]:
                fn(eng)

        @block.gpsimd
        def _(eng):
            for fn in ops["gpsimd"]:
                fn(eng)

        @block.tensor
        def _(eng):
            for fn in ops["tensor"]:
                fn(eng)

        @block.vector
        def _(eng):
            for fn in ops["vector"]:
                fn(eng)

        @block.scalar
        def _(eng):
            for fn in ops["scalar"]:
                fn(eng)

    ctx.close()
    nc._cnt_debug = dict(cnt)
    return nc



def _host_layout(L0_attn_x, L0_attn_w_qkv, L0_attn_w_out, L0_ffn_router_w,
                 L0_ffn_router_b, L0_ffn_expert_w, L0_ffn_expert_b):
    x = np.ascontiguousarray(L0_attn_x, dtype=np.float32)
    wqkv_f = np.ascontiguousarray(L0_attn_w_qkv, dtype=np.float32)
    wout_f = np.ascontiguousarray(L0_attn_w_out, dtype=np.float32)
    rw_f = np.ascontiguousarray(L0_ffn_router_w, dtype=np.float32)
    rb_f = np.ascontiguousarray(L0_ffn_router_b, dtype=np.float32)
    ew_f = np.ascontiguousarray(L0_ffn_expert_w, dtype=np.float32)
    eb_f = np.ascontiguousarray(L0_ffn_expert_b, dtype=np.float32)

    wq4 = wqkv_f.reshape(H, 3, NHEAD, 128)
    wqkv_host = np.empty((NHEAD, 128, 2 * HC, 128), dtype=np.float32)
    wv_host = np.empty((NHEAD, 128, HC, 128), dtype=np.float32)
    for head in range(NHEAD):
        for src_role, dev_role in ((1, 0), (0, 1)):
            blk = wq4[:, src_role, head, :].reshape(HC, 128, 128).transpose(1, 0, 2)
            wqkv_host[head, :, dev_role * HC:(dev_role + 1) * HC, :] = blk
        wv_host[head] = wq4[:, 2, head, :].reshape(HC, 128, 128).transpose(1, 0, 2)
    wqkv_host = wqkv_host.reshape(NHEAD, 128, 2 * HC * 128)
    wv_host = wv_host.reshape(NHEAD, 128, HC * 128)

    wout_host = np.ascontiguousarray(
        wout_f.reshape(HC, 128, 16, 128).transpose(2, 1, 0, 3).reshape(16, 128, HC * 128))

    rwr = rw_f.reshape(O, E, H)
    rw_host = np.ascontiguousarray(
        rwr.reshape(NSPAN, SO, E, HC, 128).transpose(0, 4, 3, 2, 1).reshape(NSPAN, 128, HC * E * SO))
    rb_host = np.ascontiguousarray(
        rb_f.reshape(O, E).reshape(NSPAN, SO, E).transpose(0, 2, 1))

    ew_host = np.ascontiguousarray(
        ew_f.reshape(E, 16, 128, HC, 128).transpose(1, 4, 0, 3, 2).reshape(16, 128, E * HC * 128))

    kk = np.arange(128)[:, None]
    qq = np.arange(512)[None, :]
    tri_host = np.zeros((4, 128, 512), dtype=np.float32)
    for dd in range(4):
        tri_host[dd] = np.where(qq < kk + dd * 128, TRI_NEG, 0.0).astype(np.float32)
    tri_host = tri_host.astype(bf16)
    ident_host = np.eye(128, dtype=bf16)
    ue_host = np.zeros((E, E * 128), dtype=bf16)
    for e_ in range(E):
        ue_host[e_, e_ * 128:(e_ + 1) * 128] = 1.0

    in_maps = []
    for c in range(8):
        b_, j = c // 4, c % 4
        t1 = (j + 1) * D
        shift = S - t1
        xt_loc = np.zeros((H, S), dtype=np.float32)
        xt_loc[:, shift:] = x[b_, :t1, :].T
        xt_own_h = np.ascontiguousarray(
            x[b_, j * D:(j + 1) * D, :].T.reshape(16, 128, D))
        kb = np.empty((S,), dtype=np.float32)
        kb[:shift] = KB_NEG
        kb[shift:] = -CSHIFT
        in_maps.append({
            "xt": xt_loc,
            "xt_own": xt_own_h,
            "wqkv": wqkv_host,
            "wv": wv_host,
            "wout": wout_host,
            "rw": rw_host,
            "rb": rb_host,
            "ew": ew_host,
            "eb": eb_f,
            "tri": tri_host,
            "kbias": np.ascontiguousarray(kb.reshape(NKC, 128).T),
            "ident": ident_host,
            "ue": ue_host,
        })
    return in_maps


def kernel(L0_attn_x, L0_attn_w_qkv, L0_attn_w_out, L0_ffn_router_w,
           L0_ffn_router_b, L0_ffn_expert_w, L0_ffn_expert_b, _debug=False):
    in_maps = _host_layout(L0_attn_x, L0_attn_w_qkv, L0_attn_w_out,
                           L0_ffn_router_w, L0_ffn_router_b,
                           L0_ffn_expert_w, L0_ffn_expert_b)
    nc = build_program(debug=_debug)
    res = run_bass_kernel_spmd(nc, in_maps, core_ids=list(range(8)))
    out = np.empty((B, S, H), dtype=np.float32)
    for c in range(8):
        b_, j = c // 4, c % 4
        o = res.results[c]["out_t"]
        out[b_, j * D:(j + 1) * D, :] = o.transpose(2, 0, 1).reshape(D, H)
    if _debug:
        return out, res
    return out


# version 08
# speedup vs baseline: 1.0662x; 1.0662x over previous
import numpy as np
import ml_dtypes
from contextlib import ExitStack
import concourse.bass as bass
import concourse.mybir as mybir
from concourse.bass_utils import run_bass_kernel_spmd

F32 = mybir.dt.float32
BF16 = mybir.dt.bfloat16
AF = mybir.ActivationFunctionType
ALU = mybir.AluOpType
AX = mybir.AxisListType

H = 2048
S = 4096
B = 2
NHEAD = 16
E = 8
O = 2048
HC = 16
D = 1024
NTB = D // 128
NQT = 2
NKC = S // 128
SCALE = 1.0 / float(np.sqrt(128.0))
CSHIFT = 12.0
TRI_NEG = -1.0e12
KB_NEG = -1.0e10
EPS = 1e-5
BIGM = 120.0
SO = 64
NSPAN = O // SO

bf16 = ml_dtypes.bfloat16


def build_program(debug=False, detect_races=True):
    nc = bass.Bass("TRN2", target_bir_lowering=False, detect_race_conditions=detect_races)

    xt = nc.dram_tensor("xt", [H, S], F32, kind="ExternalInput").ap()
    xt_own = nc.dram_tensor("xt_own", [16, 128, D], F32, kind="ExternalInput").ap()
    wqkv = nc.dram_tensor("wqkv", [NHEAD, 128, 2 * HC * 128], F32, kind="ExternalInput").ap()
    wv = nc.dram_tensor("wv", [NHEAD, 128, HC * 128], F32, kind="ExternalInput").ap()
    v_hbm_t = nc.dram_tensor("v_hbm", [NKC, 128, NHEAD, 128], BF16)
    v_hbm = v_hbm_t.ap()
    wout = nc.dram_tensor("wout", [16, 128, HC * 128], F32, kind="ExternalInput").ap()
    rw = nc.dram_tensor("rw", [NSPAN, 128, HC * E * SO], F32, kind="ExternalInput").ap()
    rb = nc.dram_tensor("rb", [NSPAN, E, SO], F32, kind="ExternalInput").ap()
    ew = nc.dram_tensor("ew", [16, 128, E * HC * 128], F32, kind="ExternalInput").ap()
    eb = nc.dram_tensor("eb", [E, O], F32, kind="ExternalInput").ap()
    tri = nc.dram_tensor("tri", [4, 128, 512], BF16, kind="ExternalInput").ap()
    kbias = nc.dram_tensor("kbias", [128, NKC], F32, kind="ExternalInput").ap()
    ident = nc.dram_tensor("ident", [128, 128], BF16, kind="ExternalInput").ap()
    ue = nc.dram_tensor("ue", [E, E * 128], BF16, kind="ExternalInput").ap()
    out_t = nc.dram_tensor("out_t", [16, 128, D], F32, kind="ExternalOutput").ap()
    attn_hbm = nc.dram_tensor("attn_hbm", [NHEAD, 128, D], BF16).ap()
    xn_hbm = nc.dram_tensor("xn_hbm", [HC, 128, D], BF16).ap()

    dbg = {}
    if debug:
        dbg["xnT"] = nc.dram_tensor("dbg_xnT", [HC, 128, S], BF16, kind="ExternalOutput").ap()
        dbg["attnT"] = nc.dram_tensor("dbg_attnT", [NHEAD, 128, D], BF16, kind="ExternalOutput").ap()
        dbg["c"] = nc.dram_tensor("dbg_c", [NTB, 128, E], F32, kind="ExternalOutput").ap()
        dbg["cparts"] = nc.dram_tensor("dbg_cparts", [128, NTB * 8 * E], F32, kind="ExternalOutput").ap()
        dbg["lgq"] = nc.dram_tensor("dbg_lgq", [128, NTB * E * 256], BF16, kind="ExternalOutput").ap()
        dbg["rwsb"] = nc.dram_tensor("dbg_rwsb", [128, HC * E * SO], BF16, kind="ExternalOutput").ap()

    ENGS = ("sync", "gpsimd", "tensor", "vector", "scalar")
    ops = {e: [] for e in ENGS}
    cnt = {"dS": 0, "dG": 0, "dC": 0, "dEB": 0, "dXN": 0,
           "sX0": 0, "sX1": 0, "sN0": 0, "sN1": 0, "sN2": 0,
           "sW0": 0, "sW1": 0, "sR0": 0, "sR1": 0, "sE0": 0, "sE1": 0,
           "sF0": 0, "sF1": 0, "dOS": 0, "sN3": 0, "sVS0": 0, "sVS1": 0,
           "T": 0, "V": 0, "Sc": 0}

    ctx = ExitStack()
    sems = {s: ctx.enter_context(nc.semaphore("sem_" + s)) for s in cnt}

    def emit(eng, fn):
        ops[eng].append(fn)

    def wait(eng, sem, val):
        if val is None or val <= 0:
            return
        emit(eng, lambda e, s=sems[sem], v=val: e.wait_ge(s, v))

    def dma(eng, dst, src, key):
        cnt[key] += 16
        v = cnt[key]
        emit(eng, lambda e, d=dst, s=src, sm=sems[key]: e.dma_start(d, s).then_inc(sm, 16))
        return v

    def t_mm(psum, lhsT, rhs, start, stop, inc=False):
        if inc:
            cnt["T"] += 1
            emit("tensor", lambda e, o=psum, l=lhsT, r=rhs, st=start, sp=stop, sm=sems["T"]:
                 e.matmul(o, l, r, start=st, stop=sp).then_inc(sm, 1))
            return cnt["T"]
        emit("tensor", lambda e, o=psum, l=lhsT, r=rhs, st=start, sp=stop:
             e.matmul(o, l, r, start=st, stop=sp))
        return None

    def t_tr(psum, src, idn, inc=False):
        if inc:
            cnt["T"] += 1
            emit("tensor", lambda e, o=psum, s=src, i=idn, sm=sems["T"]:
                 e.transpose(o, s, i).then_inc(sm, 1))
            return cnt["T"]
        emit("tensor", lambda e, o=psum, s=src, i=idn: e.transpose(o, s, i))
        return None

    def v_op(name, *args, inc=False, **kw):
        if inc:
            cnt["V"] += 1
            emit("vector", lambda e, n=name, a=args, k=kw, sm=sems["V"]:
                 getattr(e, n)(*a, **k).then_inc(sm, 1))
            return cnt["V"]
        emit("vector", lambda e, n=name, a=args, k=kw: getattr(e, n)(*a, **k))
        return None

    def s_op(name, *args, inc=False, **kw):
        if inc:
            cnt["Sc"] += 1
            emit("scalar", lambda e, n=name, a=args, k=kw, sm=sems["Sc"]:
                 getattr(e, n)(*a, **k).then_inc(sm, 1))
            return cnt["Sc"]
        emit("scalar", lambda e, n=name, a=args, k=kw: getattr(e, n)(*a, **k))
        return None

    SB = lambda st, name, shape, dt=F32, **kw: st.enter_context(nc.sbuf_tensor(name, shape, dt, **kw))
    PS = lambda st, name, shape, dt=F32: st.enter_context(nc.psum_tensor(name, shape, dt))

    ident_sb = SB(ctx, "ident_sb", [128, 128], BF16, side="right")
    tri_sb = SB(ctx, "tri_sb", [128, 4 * 512], BF16, side="right")
    kb_sb = SB(ctx, "kb_sb", [128, NKC], F32, side="right")
    ones1 = SB(ctx, "ones1", [1, 128], BF16, side="right")
    ones1f = SB(ctx, "ones1f", [1, 128], F32, side="right")
    eps_sb = SB(ctx, "eps_sb", [1, 1], F32, side="right")
    onesc = SB(ctx, "onesc", [128, 1], BF16, side="right")
    stg2k = [SB(ctx, "stg2k%d" % i, [128, 2048], F32, side="right") for i in range(2)]

    dma("sync", ident_sb[:, :], ident, "dC")
    dma("sync", tri_sb[:, :].rearrange("p (n q) -> p n q", n=4), tri.rearrange("n p q -> p n q"), "dC")
    dma("sync", kb_sb[:, :], kbias, "dC")
    setup_dC = cnt["dC"]
    v_op("memset", ones1[:, :], 1.0)
    v_op("memset", ones1f[:, :], 1.0)
    v_op("memset", eps_sb[:, :], EPS)
    v_ones = v_op("memset", onesc[:, :], 1.0, inc=True)

    xnT_st = ExitStack()
    xnT = SB(xnT_st, "xnT", [128, HC * S], BF16, side="left")

    ln_st = ExitStack()
    xstgf = [SB(ln_st, "xstgf%d" % i, [128, 1024]) for i in range(3)]
    xstrip = [SB(ln_st, "xstrip%d" % i, [128, 1024], BF16) for i in range(2)]
    sqstrip = [SB(ln_st, "sqstrip%d" % i, [128, 1024], BF16) for i in range(2)]
    nstr = [SB(ln_st, "nstr%d" % i, [128, 512]) for i in range(3)]
    tscr = [SB(ln_st, "tscr%d" % i, [128, 512]) for i in range(2)]
    m_bc = [SB(ln_st, "m_bc%d" % i, [128, 512]) for i in range(2)]
    r_bc = [SB(ln_st, "r_bc%d" % i, [128, 512]) for i in range(2)]
    mrow = SB(ln_st, "mrow", [1, 1024], F32)
    sqrow = SB(ln_st, "sqrow", [1, 1024], F32)
    r_mn = SB(ln_st, "r_mn", [1, 512], F32)
    r_msq = SB(ln_st, "r_msq", [1, 512], F32)
    r_var = SB(ln_st, "r_var", [1, 512], F32)
    r_vb = SB(ln_st, "r_vb", [1, 512], F32)
    r_rstd = SB(ln_st, "r_rstd", [1, 512], F32)
    onescf = SB(ln_st, "onescf", [128, 1], F32)
    ln_ps = ExitStack()
    p_stat = [PS(ln_ps, "p_stat%d" % i, [1, 512]) for i in range(4)]
    p_bc0 = PS(ln_ps, "p_bc0", [128, 512])
    p_bc1 = PS(ln_ps, "p_bc1", [128, 512])

    v_onesf = v_op("memset", onescf[:, :], 1.0, inc=True)
    strip_T = {}
    strip_V = {}
    norm_V = {}
    norm_vn = {}
    stat_evac_V = 0
    for qr in range(4):
        for hc in range(HC):
            li = qr * HC + hc
            ib = li % 3
            sq = li % 2
            skey = "sX%d" % (ib % 2) if False else ("sX0" if ib == 0 else ("sX1" if ib == 1 else "sN0"))
            wait("sync", "V", strip_V.get(li - 3))
            tok = dma("sync", xstgf[ib][:, :],
                      xt[hc * 128:(hc + 1) * 128, qr * 1024:(qr + 1) * 1024], skey)
            wait("vector", "T", strip_T.get(li - 2))
            wait("vector", skey, tok)
            v_op("tensor_copy", xstrip[sq][:, :], xstgf[ib][:, :])
            vsq = v_op("tensor_mul", sqstrip[sq][:, :], xstrip[sq][:, :], xstrip[sq][:, :], inc=True)
            strip_V[li] = vsq
            wait("tensor", "V", vsq)
            if li == 0:
                wait("tensor", "V", v_ones)
            if hc == 0:
                wait("tensor", "V", stat_evac_V)
            for tt in range(2):
                t_mm(p_stat[tt][:, :], onesc[:, :], xstrip[sq][:, tt * 512:(tt + 1) * 512],
                     start=(hc == 0), stop=(hc == HC - 1))
                t_mm(p_stat[2 + tt][:, :], onesc[:, :], sqstrip[sq][:, tt * 512:(tt + 1) * 512],
                     start=(hc == 0), stop=(hc == HC - 1), inc=(tt == 1))
            strip_T[li] = cnt["T"]
        wait("vector", "T", cnt["T"])
        for tt in range(2):
            v_op("tensor_copy", mrow[0:1, tt * 512:(tt + 1) * 512], p_stat[tt][:, :])
            v_op("tensor_copy", sqrow[0:1, tt * 512:(tt + 1) * 512], p_stat[2 + tt][:, :],
                 inc=(tt == 1))
        stat_evac_V = cnt["V"]

        for ttq in range(2):
            tt = qr * 2 + ttq
            off = ttq * 512
            bb = tt % 2
            wait("vector", "Sc", norm_V.get(("rs", tt - 1)))
            v_op("tensor_scalar_mul", r_mn[0:1, :], mrow[0:1, off:off + 512], 1.0 / 2048.0)
            v_op("tensor_mul", r_msq[0:1, :], r_mn[0:1, :], r_mn[0:1, :])
            v_op("tensor_scalar_mul", r_var[0:1, :], sqrow[0:1, off:off + 512], 1.0 / 2048.0)
            v_var = v_op("tensor_sub", r_vb[0:1, :], r_var[0:1, :], r_msq[0:1, :], inc=True)
            wait("scalar", "V", v_var)
            s_op("activation", r_msq[0:1, :], r_vb[0:1, :], AF.Ln, bias=eps_sb[0:1, :])
            s_rstd = s_op("activation", r_rstd[0:1, :], r_msq[0:1, :], AF.Exp, scale=-0.5, inc=True)
            norm_V[("rs", tt)] = s_rstd
            wait("tensor", "Sc", s_rstd)
            wait("tensor", "V", norm_V.get(tt - 1))
            t_mm(p_bc0[:, :], ones1f[:, :], r_mn[0:1, :], start=True, stop=True)
            tv = t_mm(p_bc1[:, :], ones1f[:, :], r_rstd[0:1, :], start=True, stop=True, inc=True)
            wait("vector", "T", tv)
            v_op("tensor_copy", m_bc[bb][:, :], p_bc0[:, :])
            norm_V[tt] = v_op("tensor_copy", r_bc[bb][:, :], p_bc1[:, :], inc=True)
            for hc in range(HC):
                nidx = tt * HC + hc
                ib = nidx % 3
                sc = nidx % 2
                nkey = "sN%d" % (1 + ib)
                wait("sync", "V", norm_vn.get(nidx - 3))
                tok = dma("sync", nstr[ib][:, :],
                          xt[hc * 128:(hc + 1) * 128, tt * 512:(tt + 1) * 512], nkey)
                wait("vector", nkey, tok)
                v_op("tensor_sub", tscr[sc][:, :], nstr[ib][:, :], m_bc[bb][:, :])
                vn = v_op("tensor_mul", xnT[:, hc * S + tt * 512: hc * S + (tt + 1) * 512],
                          tscr[sc][:, :], r_bc[bb][:, :], inc=True)
                norm_vn[nidx] = vn
                if hc == HC - 1:
                    norm_V[tt] = vn
    ln_done_V = cnt["V"]
    ln_ps.close()

    wait("sync", "V", ln_done_V)
    xn_hbm_tok = 0
    for hc in range(HC):
        xn_hbm_tok = dma("sync", xn_hbm[hc], xnT[:, hc * S + 3072: hc * S + 4096], "dS")
    if debug:
        for hc in range(HC):
            dma("sync", dbg["xnT"][hc], xnT[:, hc * S:(hc + 1) * S], "dS")
    ln_st.close()

    vp_st = ExitStack()
    wv_half = SB(vp_st, "wv_half", [128, HC * 8 * 128], BF16)
    vstrip = [SB(vp_st, "vstrip%d" % i, [128, 512], BF16) for i in range(2)]
    vp_ps = ExitStack()
    p_vp = [PS(vp_ps, "p_vp%d" % i, [128, 512]) for i in range(2)]

    wait("tensor", "dC", setup_dC)
    wait("scalar", "dC", setup_dC)
    wait("vector", "dC", setup_dC)

    vpre_half_dS = {}
    vp_ev = {}
    vp_store = {}
    vp_conv = {}
    vgi = 0
    for half in range(2):
        cvl = None
        for h8 in range(8):
            head = half * 8 + h8
            fkey = "sF%d" % (h8 % 2)
            wait("sync", "V", vp_conv.get((half, h8 - 2)))
            if half > 0 and h8 < 2:
                wait("sync", "V", vp_conv.get((half - 1, 6 + h8)))
            ftok = dma("sync", stg2k[h8 % 2][:, :], wv[head], fkey)
            wait("vector", fkey, ftok)
            if half > 0 and h8 == 0:
                wait("vector", "T", cnt["T"])
            if head == 0:
                wait("vector", "V", ln_done_V)
            cvl = v_op("tensor_copy",
                       wv_half[:, :].rearrange("p (hc h c) -> p hc h c", h=8, c=128)[:, :, h8, :],
                       stg2k[h8 % 2][:, :].rearrange("p (hc c) -> p hc c", c=128), inc=True)
            vp_conv[(half, h8)] = cvl
        wait("tensor", "V", cvl)
        if half == 0:
            wait("tensor", "V", ln_done_V)
        for tc in range(NKC):
            for grp in range(2):
                gi = vgi
                vgi += 1
                pv = p_vp[gi % 2]
                wait("tensor", "Sc", vp_ev.get(gi - 2))
                for hc in range(HC):
                    t_mm(pv[:, :], xnT[:, hc * S + tc * 128:(hc + 1 - 1) * S + hc * S + (tc + 1) * 128]
                         if False else xnT[:, hc * S + tc * 128: hc * S + (tc + 1) * 128],
                         wv_half[:, (hc * 8 + grp * 4) * 128:(hc * 8 + (grp + 1) * 4) * 128],
                         start=(hc == 0), stop=(hc == HC - 1), inc=(hc == HC - 1))
                wait("scalar", "T", cnt["T"])
                wait("scalar", "sVS%d" % (gi % 2), vp_store.get(gi - 2))
                ev = s_op("copy", vstrip[gi % 2][:, :], pv[:, :], inc=True)
                vp_ev[gi] = ev
                wait("sync", "Sc", ev)
                vp_store[gi] = dma("sync",
                                   v_hbm[tc][:, half * 8 + grp * 4: half * 8 + (grp + 1) * 4, :],
                                   vstrip[gi % 2][:, :].rearrange("p (h d) -> p h d", d=128),
                                   "sVS%d" % (gi % 2))
        vpre_half_dS[half] = (cnt["sVS0"], cnt["sVS1"])
    vp_ps.close()
    vp_st.close()

    at_st = ExitStack()
    NWC = 2 * HC
    wg_sb = [SB(at_st, "wg_sb%d" % i, [128, NWC * 128], BF16) for i in range(2)]
    ktbuf = SB(at_st, "ktbuf", [128, S], BF16)
    qtbuf = SB(at_st, "qtbuf", [128, D], BF16)
    vaug = [SB(at_st, "vaug%d" % i, [128, NKC * 129], BF16) for i in range(2)]
    expt = [SB(at_st, "expt%d" % i, [128, 512], BF16) for i in range(4)]
    mskscr = [SB(at_st, "mskscr%d" % i, [128, 512]) for i in range(2)]
    aosb = [SB(at_st, "aosb%d" % i, [128, 128], BF16) for i in range(2)]
    rec_sb = SB(at_st, "rec_sb", [128, 8], F32)
    astrip = SB(at_st, "astrip", [128, D], BF16)
    at_ps = ExitStack()
    p_stage = PS(at_ps, "p_stage", [128, 512])
    p_sc = [PS(at_ps, "p_sc%d" % i, [128, 512]) for i in range(2)]
    p_av = [PS(at_ps, "p_av%d" % i, [128, 129]) for i in range(4)]
    p_tr = PS(at_ps, "p_tr", [128, 128], BF16)
    stage_ps = [p_stage, p_sc[0]]

    def wgc(buf, role, hc):
        i = role * HC + hc
        return wg_sb[buf][:, i * 128:(i + 1) * 128]

    head_T = {}
    ev_Sc = {}
    stage_ev = [None, None]
    stage_i = 0
    av_T = {}
    tr_T = {}
    store_tok = {}
    wg_conv = {}
    va_tok = {}

    for head in range(NHEAD):
        gb = head % 2
        cvw = None
        for sl in range(2):
            fkey = "sF%d" % (sl % 2)
            if head == 0:
                wait("sync", "V", vp_conv.get((1, 6 + sl)))
            wait("sync", "V", wg_conv.get((head - 1, sl)))
            ftok = dma("sync", stg2k[sl][:, :], wqkv[head][:, sl * 2048:(sl + 1) * 2048], fkey)
            wait("vector", fkey, ftok)
            wait("vector", "T", head_T.get(head - 2))
            cvw = v_op("tensor_copy", wg_sb[gb][:, sl * 2048:(sl + 1) * 2048],
                       stg2k[sl][:, :], inc=True)
            wg_conv[(head, sl)] = cvw
        wait("sync", "sVS0", vpre_half_dS[head // 8][0])
        wait("sync", "sVS1", vpre_half_dS[head // 8][1])
        wait("sync", "T", head_T.get(head - 2))
        va_tok[head] = dma("sync", vaug[gb][:, :].rearrange("p (k n) -> p k n", n=129)[:, :, 0:128],
                           bass.AP(v_hbm_t, head * 128,
                                   [[NHEAD * 128, 128], [128 * NHEAD * 128, NKC], [1, 128]]),
                           "sW%d" % gb)
        wait("vector", "sW%d" % gb, va_tok[head])
        v_onesc = v_op("memset", vaug[gb][:, :].rearrange("p (k n) -> p k n", n=129)[:, :, 128:129],
                       1.0, inc=True)
        wait("tensor", "V", cvw)
        if head == 0:
            wait("tensor", "V", ln_done_V)
        for kt in range(8):
            sbk = stage_ps[stage_i % 2]
            wait("tensor", "Sc", stage_ev[stage_i % 2])
            for hc in range(HC):
                t_mm(sbk[:, :], wgc(gb, 0, hc), xnT[:, hc * S + kt * 512: hc * S + (kt + 1) * 512],
                     start=(hc == 0), stop=(hc == HC - 1), inc=(hc == HC - 1))
            wait("scalar", "T", cnt["T"])
            last_stage_ev = s_op("copy", ktbuf[:, kt * 512:(kt + 1) * 512], sbk[:, :], inc=True)
            stage_ev[stage_i % 2] = last_stage_ev
            stage_i += 1
        for qt in range(NQT):
            sbk = stage_ps[stage_i % 2]
            wait("tensor", "Sc", stage_ev[stage_i % 2])
            for hc in range(HC):
                t_mm(sbk[:, :], wgc(gb, 1, hc),
                     xnT[:, hc * S + 3072 + qt * 512: hc * S + 3072 + (qt + 1) * 512],
                     start=(hc == 0), stop=(hc == HC - 1), inc=(hc == HC - 1))
            wait("scalar", "T", cnt["T"])
            last_stage_ev = s_op("copy", qtbuf[:, qt * 512:(qt + 1) * 512], sbk[:, :], inc=True)
            stage_ev[stage_i % 2] = last_stage_ev
            stage_i += 1
        wait("tensor", "Sc", cnt["Sc"])
        wait("tensor", "V", cnt["V"])

        for qt in range(NQT):
            nkc = 28 + 4 * qt
            diag0 = 24 + 4 * qt

            sc_ps = [p_sc[0], p_sc[1], p_stage]

            def emit_score(kc):
                pb = kc % 3
                wait("tensor", "Sc", ev_Sc.get(("exp", head, qt, kc - 3)))
                tv = t_mm(sc_ps[pb][:, :], ktbuf[:, kc * 128:(kc + 1) * 128],
                          qtbuf[:, qt * 512:(qt + 1) * 512], start=True, stop=True, inc=True)
                dd = kc - diag0
                if dd >= 0:
                    wait("vector", "T", tv)
                    wait("vector", "Sc", ev_Sc.get(("exp", head, qt, kc - 2)))
                    vmask = v_op("tensor_add", mskscr[kc % 2][:, :], sc_ps[pb][:, :],
                                 tri_sb[:, dd * 512:(dd + 1) * 512], inc=True)
                    wait("scalar", "V", vmask)
                    src = mskscr[kc % 2][:, :]
                else:
                    wait("scalar", "T", tv)
                    src = sc_ps[pb][:, :]
                eb_ = kc % 4
                wait("scalar", "T", av_T.get((head, qt, kc - 4)))
                ex = s_op("activation", expt[eb_][:, :], src, AF.Exp,
                          bias=kb_sb[:, kc:kc + 1], scale=SCALE, inc=True)
                ev_Sc[("exp", head, qt, kc)] = ex

            def emit_av(kc):
                eb_ = kc % 4
                wait("tensor", "Sc", ev_Sc[("exp", head, qt, kc)])
                for qs in range(4):
                    t_mm(p_av[qs][:, :], expt[eb_][:, qs * 128:(qs + 1) * 128],
                         vaug[gb][:, kc * 129:(kc + 1) * 129],
                         start=(kc == 0), stop=(kc == nkc - 1), inc=(qs == 3))
                av_T[(head, qt, kc)] = cnt["T"]

            emit_score(0)
            emit_score(1)
            for kc in range(2, nkc):
                emit_score(kc)
                emit_av(kc - 2)
            emit_av(nkc - 2)
            emit_av(nkc - 1)
            wait("vector", "T", cnt["T"])
            for qs in range(4):
                v_op("reciprocal", rec_sb[:, qs:qs + 1], p_av[qs][:, 128:129],
                     inc=(qs == 3))
            wait("scalar", "V", cnt["V"])
            for qs in range(4):
                ob = qs % 2
                wait("scalar", "T", tr_T.get((head, qt, qs - 2)))
                sa = s_op("activation", aosb[ob][:, :], p_av[qs][:, 0:128], AF.Copy,
                          scale=rec_sb[:, qs:qs + 1], inc=True)
                wait("tensor", "Sc", sa)
                ttr = t_tr(p_tr[:, :], aosb[ob][:, :], ident_sb[:, :], inc=True)
                tr_T[(head, qt, qs)] = ttr
                wait("vector", "T", ttr)
                vc = v_op("tensor_copy", astrip[:, qt * 512 + qs * 128: qt * 512 + (qs + 1) * 128],
                          p_tr[:, :], inc=True)
                wait("tensor", "V", vc)
        stage_ev[1] = max(stage_ev[1] or 0, ev_Sc.get(("exp", head, NQT - 1, 30), 0),
                          ev_Sc.get(("exp", head, NQT - 1, 31), 0))
        stage_ev[0] = max(stage_ev[0] or 0, ev_Sc.get(("exp", head, NQT - 1, 29), 0),
                          ev_Sc.get(("exp", head, NQT - 1, 31), 0))
        wait("sync", "V", cnt["V"])
        store_tok[head] = dma("sync", attn_hbm[head], astrip[:, :], "dS")
        wait("vector", "dS", store_tok[head])
        head_T[head] = cnt["T"]

    att_T_done = cnt["T"]
    att_store_dS = cnt["dS"]
    at_ps.close()
    at_st.close()
    xnT_st.close()

    rt_keep = ExitStack()
    xnT_own = SB(rt_keep, "xnT_own", [128, HC * D], BF16)
    cbc = SB(rt_keep, "cbc", [128, E * D], BF16)
    cT8 = SB(rt_keep, "cT8", [E, D], BF16)
    c_fin = SB(rt_keep, "c_fin", [128, NTB * E], F32)
    cparts = SB(rt_keep, "cparts", [128, NTB * 8 * E], F32)
    ue_sb = SB(rt_keep, "ue_sb", [E, E * 128], BF16)
    rt_st = ExitStack()
    rw_sb = [SB(rt_st, "rw_sb%d" % i, [128, HC * E * SO], BF16) for i in range(2)]
    rb_sb = [SB(rt_st, "rb_sb%d" % i, [1, E * SO], BF16) for i in range(2)]
    lgq = [SB(rt_st, "lgq%d" % i, [128, NTB * E * 256], BF16) for i in range(2)]
    m1s = SB(rt_st, "m1s", [128, 7 * 256], BF16)
    m2s = SB(rt_st, "m2s", [128, 7 * 256], BF16)
    geA = SB(rt_st, "geA", [128, E * 256], BF16)
    lms = SB(rt_st, "lms", [128, E * 256], BF16)
    p12s = SB(rt_st, "p12s", [128, 256], BF16)
    s1s = SB(rt_st, "s1s", [128, E * 256], BF16)
    sigs = SB(rt_st, "sigs", [128, E * 256], BF16)
    ctree = SB(rt_st, "ctree", [128, 6 * NTB * E], F32)
    cbf = SB(rt_st, "cbf", [128, NTB * E], BF16)
    rt_ps = ExitStack()
    p_lg = [PS(rt_ps, "p_lg%d" % i, [128, 512]) for i in range(4)]
    p_ct = PS(rt_ps, "p_ct", [E, 128], BF16)
    p_cb = [PS(rt_ps, "p_cb%d" % i, [128, 512]) for i in range(2)]

    wait("gpsimd", "T", att_T_done)
    wait("gpsimd", "dS", att_store_dS)
    wait("sync", "T", att_T_done)
    wait("sync", "dS", att_store_dS)
    d_ue = dma("sync", ue_sb[:, :], ue, "dC")
    xo_tok = 0
    for hc in range(HC):
        xo_tok = dma("gpsimd", xnT_own[:, hc * D:(hc + 1) * D], xn_hbm[hc], "dXN")

    rw_T = {}
    rstg_hist = []
    lg_Sc = {}
    lgq_done_Sc = {}
    quad_V = {}
    ge2_done = {}

    def emit_routing_tb(q, tb):
        qb = q % 2
        if tb == 0:
            wait("vector", "Sc", lgq_done_Sc[q])
        if True:
            L = lgq[qb][:, tb * E * 256:(tb + 1) * E * 256].rearrange("p (e o) -> p e o", o=256)
            M1 = m1s[:, :].rearrange("p (e o) -> p e o", o=256)
            M2 = m2s[:, :].rearrange("p (e o) -> p e o", o=256)
            v_op("tensor_max", M1[:, 0:4], L[:, 0:4], L[:, 4:8])
            v_op("tensor_max", M1[:, 4:6], M1[:, 0:2], M1[:, 2:4])
            v_op("tensor_max", M1[:, 6:7], M1[:, 4:5], M1[:, 5:6])
            GE = geA[:, :].rearrange("p (e o) -> p e o", o=256)
            v_op("tensor_tensor", GE, L, M1[:, 6:7].to_broadcast([128, E, 256]), op=ALU.is_ge)
            v_op("tensor_scalar_mul", lms[:, :], geA[:, :], BIGM)
            LM = lms[:, :].rearrange("p (e o) -> p e o", o=256)
            v_op("tensor_sub", LM, L, LM)
            v_op("tensor_max", M2[:, 0:4], LM[:, 0:4], LM[:, 4:8])
            v_op("tensor_max", M2[:, 4:6], M2[:, 0:2], M2[:, 2:4])
            v_op("tensor_max", M2[:, 6:7], M2[:, 4:5], M2[:, 5:6])
            v_op("tensor_add", p12s[:, :], M1[:, 6:7].squeeze(1), M2[:, 6:7].squeeze(1))
            v_op("tensor_scalar_mul", s1s[:, :], lgq[qb][:, tb * E * 256:(tb + 1) * E * 256], 2.0)
            S1 = s1s[:, :].rearrange("p (e o) -> p e o", o=256)
            vs1 = v_op("tensor_sub", S1, S1,
                       p12s[:, :].unsqueeze(1).to_broadcast([128, E, 256]), inc=True)
            wait("scalar", "V", vs1)
            ssig = s_op("activation", sigs[:, :], s1s[:, :], AF.Sigmoid, inc=True)
            v_op("tensor_tensor", GE, L, M2[:, 6:7].to_broadcast([128, E, 256]), op=ALU.is_ge)
            wait("vector", "Sc", ssig)
            v_op("tensor_mul", s1s[:, :], sigs[:, :], geA[:, :])
            quad_V[q] = v_op(
                "reduce_sum",
                cparts[:, (tb * 8 + q) * E:(tb * 8 + q + 1) * E].unsqueeze(2),
                s1s[:, :].rearrange("p (e o) -> p e o", o=256), axis=AX.X, inc=True)

    for q in range(8):
        for s4 in range(4):
            sp = q * 4 + s4
            rbuf = sp % 2
            rcv = None
            BISECT_CAST = False
            if BISECT_CAST:
                rkeyw = "sR%d" % rbuf
                wait("gpsimd", "T", rw_T.get(sp - 2))
                dma("gpsimd", rw_sb[rbuf][:, :], rw[sp], rkeyw)
            else:
                for sl in range(4):
                    fkey = "sF%d" % (sl % 2)
                    if len(rstg_hist) >= 2:
                        wait("sync", "Sc", rstg_hist[-2])
                    ftok = dma("sync", stg2k[sl % 2][:, :],
                               rw[sp][:, sl * 2048:(sl + 1) * 2048], fkey)
                    wait("scalar", fkey, ftok)
                    wait("scalar", "T", rw_T.get(sp - 2))
                    rcv = s_op("copy", rw_sb[rbuf][:, sl * 2048:(sl + 1) * 2048],
                               stg2k[sl % 2][:, :], inc=True)
                    rstg_hist.append(rcv)
            rkey = "sR%d" % rbuf
            wait("gpsimd", "T", rw_T.get(sp - 2))
            tokb = dma("gpsimd", rb_sb[rbuf][:, :].rearrange("p (e o) -> p e o", o=SO),
                       rb[sp].unsqueeze(0), rkey)
            wait("tensor", rkey, tokb)
            if rcv is not None:
                wait("tensor", "Sc", rcv)
            else:
                wait("tensor", "sR%d" % rbuf, cnt["sR%d" % rbuf])
            if sp == 0:
                wait("tensor", "dXN", xo_tok)
            for tb in range(NTB):
                li = sp * NTB + tb
                pbank = p_lg[li % 4]
                wait("tensor", "Sc", lg_Sc.get(li - 4))
                if s4 == 0 and tb == 0:
                    wait("scalar", "V", quad_V.get(q - 2))
                for hc in range(HC):
                    t_mm(pbank[:, :], xnT_own[:, hc * D + tb * 128: hc * D + (tb + 1) * 128],
                         rw_sb[rbuf][:, hc * E * SO:(hc + 1) * E * SO],
                         start=(hc == 0), stop=False)
                t_mm(pbank[:, :], ones1[:, :], rb_sb[rbuf][:, :], start=False, stop=True, inc=True)
                wait("scalar", "T", cnt["T"])
                lg_Sc[li] = s_op(
                    "copy",
                    lgq[q % 2][:, tb * E * 256:(tb + 1) * E * 256]
                    .rearrange("p (e o) -> p e o", o=256)[:, :, s4 * SO:(s4 + 1) * SO],
                    pbank[:, :].rearrange("p (e o) -> p e o", o=SO), inc=True)
            rw_T[sp] = cnt["T"]
            if q > 0:
                emit_routing_tb(q - 1, 2 * s4)
                emit_routing_tb(q - 1, 2 * s4 + 1)
        lgq_done_Sc[q] = cnt["Sc"]
    for tb in range(NTB):
        emit_routing_tb(7, tb)

    CPA = cparts[:, :].rearrange("p (t q e) -> p t q e", q=8, e=E)
    CTA = ctree[:, 0:4 * NTB * E].rearrange("p (t q e) -> p t q e", q=4, e=E)
    CTB = ctree[:, 4 * NTB * E:6 * NTB * E].rearrange("p (t q e) -> p t q e", q=2, e=E)
    v1 = v_op("tensor_add", CTA, CPA[:, :, 0:4], CPA[:, :, 4:8], inc=True)
    wait("vector", "V", v1)
    v2 = v_op("tensor_add", CTB, CTA[:, :, 0:2], CTA[:, :, 2:4], inc=True)
    wait("vector", "V", v2)
    v3 = v_op("tensor_add", c_fin[:, :].rearrange("p (t e) -> p t e", e=E),
              CTB[:, :, 0, :], CTB[:, :, 1, :], inc=True)
    wait("vector", "V", v3)
    c_done_V = cnt["V"]

    if debug:
        wait("sync", "V", c_done_V)
        for tb in range(NTB):
            dma("sync", dbg["c"][tb], c_fin[:, tb * E:(tb + 1) * E], "dS")
        dma("sync", dbg["cparts"], cparts[:, :], "dS")
        dma("sync", dbg["lgq"], lgq[1][:, :], "dS")
        dma("sync", dbg["rwsb"], rw_sb[1][:, :], "dS")

    vcb = v_op("tensor_copy", cbf[:, :], c_fin[:, :], inc=True)
    wait("tensor", "V", vcb)
    for tb in range(NTB):
        ttr = t_tr(p_ct[:, :], cbf[:, tb * E:(tb + 1) * E], ident_sb[:, :], inc=True)
        wait("vector", "T", ttr)
        vv = v_op("tensor_copy", cT8[:, tb * 128:(tb + 1) * 128], p_ct[:, :], inc=True)
        wait("tensor", "V", vv)
    wait("tensor", "dC", d_ue)
    for e_ in range(E):
        for tt in range(2):
            pcb = p_cb[(e_ * 2 + tt) % 2]
            tv = t_mm(pcb[:, :], ue_sb[:, e_ * 128:(e_ + 1) * 128],
                      cT8[:, tt * 512:(tt + 1) * 512], start=True, stop=True, inc=True)
            wait("vector", "T", tv)
            vv = v_op("tensor_copy", cbc[:, e_ * D + tt * 512: e_ * D + (tt + 1) * 512],
                      pcb[:, :], inc=True)
            wait("tensor", "V", vv)
    cbc_done_V = cnt["V"]
    rstg_last_Sc = cnt["Sc"]
    rt_ps.close()
    rt_st.close()

    ex_st = ExitStack()
    ebbf = SB(ex_st, "ebbf", [E, O], BF16)
    attnT = SB(ex_st, "attnT", [128, HC * D], BF16)
    ew_sb = [SB(ex_st, "ew_sb%d" % i, [128, E * HC * 128], BF16) for i in range(2)]
    wo_sb = [SB(ex_st, "wo_sb%d" % i, [128, HC * 128], BF16) for i in range(2)]
    xo_sb = [SB(ex_st, "xo_sb%d" % i, [128, D]) for i in range(2)]
    fin = [SB(ex_st, "fin%d" % i, [128, 512]) for i in range(4)]
    scr = [SB(ex_st, "scr%d" % i, [128, 512]) for i in range(2)]
    ex_ps = ExitStack()
    p_A = [PS(ex_ps, "p_A%d" % i, [128, 512]) for i in range(2)]
    p_B = [PS(ex_ps, "p_B%d" % i, [128, 512]) for i in range(2)]

    wait("sync", "V", c_done_V)
    wait("sync", "Sc", rstg_last_Sc)
    wait("sync", "dS", att_store_dS)
    atok = 0
    for hc in range(HC):
        atok = dma("sync", attnT[:, hc * D:(hc + 1) * D],
                   attn_hbm[hc], "dS")
    if debug:
        wait("sync", "dS", atok)
        for hh in range(NHEAD):
            dma("sync", dbg["attnT"][hh], attn_hbm[hh], "dS")

    wait("gpsimd", "V", cbc_done_V)
    d_eb = dma("gpsimd", ebbf[:, :], eb, "dEB")
    wait("tensor", "dS", atok)
    wait("tensor", "V", cbc_done_V)
    wait("tensor", "dEB", d_eb)

    ob_T = {}
    estg_hist = []
    ld_tok = {}
    pA_V = []
    pB_V = []
    fin_dS = {}

    def emit_exp_loads(ob):
        obuf = ob % 2
        ecv = None
        for sl in range(9):
            fkey = "sF%d" % (sl % 2)
            if len(estg_hist) >= 2:
                wait("sync", "Sc", estg_hist[-2])
            src = ew[ob][:, sl * 2048:(sl + 1) * 2048] if sl < 8 else wout[ob]
            ftok = dma("sync", stg2k[sl % 2][:, :], src, fkey)
            wait("scalar", fkey, ftok)
            wait("scalar", "T", ob_T.get(ob - 2))
            dst = (ew_sb[obuf][:, sl * 2048:(sl + 1) * 2048] if sl < 8
                   else wo_sb[obuf][:, :])
            ecv = s_op("copy", dst, stg2k[sl % 2][:, :], inc=True)
            estg_hist.append(ecv)
        ekey = "sE%d" % obuf
        tok_x = dma("sync", xo_sb[obuf][:, :], xt_own[ob], ekey)
        ld_tok[ob] = (tok_x, ecv)

    emit_exp_loads(0)
    for ob in range(16):
        obuf = ob % 2
        if ob + 1 < 16:
            emit_exp_loads(ob + 1)
        tok_x, ecv = ld_tok[ob]
        wait("tensor", "sE%d" % obuf, tok_x)
        wait("tensor", "Sc", ecv)
        for tt in range(2):
            idx = ob * 2 + tt
            if len(pA_V) >= 2:
                wait("tensor", "V", pA_V[-2])
            pa = p_A[idx % 2]
            t_mm(pa[:, :], ebbf[:, ob * 128:(ob + 1) * 128], cT8[:, tt * 512:(tt + 1) * 512],
                 start=True, stop=False)
            for hc in range(HC):
                t_mm(pa[:, :], wo_sb[obuf][:, hc * 128:(hc + 1) * 128],
                     attnT[:, hc * D + tt * 512: hc * D + (tt + 1) * 512],
                     start=False, stop=(hc == HC - 1), inc=(hc == HC - 1))
            wait("vector", "T", cnt["T"])
            wait("vector", "sE%d" % obuf, tok_x)
            wait("vector", "dOS", fin_dS.get(idx - 1))
            va = v_op("tensor_add", fin[0][:, :], pa[:, :], xo_sb[obuf][:, tt * 512:(tt + 1) * 512],
                      inc=True)
            pA_V.append(va)
            for e_ in range(E):
                if len(pB_V) >= 2:
                    wait("tensor", "V", pB_V[-2])
                pb = p_B[e_ % 2]
                for hc in range(HC):
                    t_mm(pb[:, :], ew_sb[obuf][:, (e_ * HC + hc) * 128:(e_ * HC + hc + 1) * 128],
                         xnT_own[:, hc * D + tt * 512: hc * D + (tt + 1) * 512],
                         start=(hc == 0), stop=(hc == HC - 1), inc=(hc == HC - 1))
                wait("vector", "T", cnt["T"])
                v_op("tensor_mul", scr[e_ % 2][:, :], pb[:, :],
                     cbc[:, e_ * D + tt * 512: e_ * D + (tt + 1) * 512])
                vb = v_op("tensor_add", fin[(e_ + 1) % 4][:, :], fin[e_ % 4][:, :],
                          scr[e_ % 2][:, :], inc=True)
                pB_V.append(vb)
            wait("gpsimd", "V", cnt["V"])
            fin_dS[idx] = dma("gpsimd", out_t[ob][:, tt * 512:(tt + 1) * 512],
                              fin[0][:, :], "dOS")
        ob_T[ob] = cnt["T"]

    wait("sync", "dS", cnt["dS"])
    wait("sync", "dOS", cnt["dOS"])
    ex_ps.close()
    ex_st.close()
    rt_keep.close()

    with nc.Block() as block:
        @block.sync
        def _(eng):
            for fn in ops["sync"]:
                fn(eng)

        @block.gpsimd
        def _(eng):
            for fn in ops["gpsimd"]:
                fn(eng)

        @block.tensor
        def _(eng):
            for fn in ops["tensor"]:
                fn(eng)

        @block.vector
        def _(eng):
            for fn in ops["vector"]:
                fn(eng)

        @block.scalar
        def _(eng):
            for fn in ops["scalar"]:
                fn(eng)

    ctx.close()
    nc._cnt_debug = dict(cnt)
    return nc



def _host_layout(L0_attn_x, L0_attn_w_qkv, L0_attn_w_out, L0_ffn_router_w,
                 L0_ffn_router_b, L0_ffn_expert_w, L0_ffn_expert_b):
    x = np.ascontiguousarray(L0_attn_x, dtype=np.float32)
    wqkv_f = np.ascontiguousarray(L0_attn_w_qkv, dtype=np.float32)
    wout_f = np.ascontiguousarray(L0_attn_w_out, dtype=np.float32)
    rw_f = np.ascontiguousarray(L0_ffn_router_w, dtype=np.float32)
    rb_f = np.ascontiguousarray(L0_ffn_router_b, dtype=np.float32)
    ew_f = np.ascontiguousarray(L0_ffn_expert_w, dtype=np.float32)
    eb_f = np.ascontiguousarray(L0_ffn_expert_b, dtype=np.float32)

    wq4 = wqkv_f.reshape(H, 3, NHEAD, 128)
    wqkv_host = np.empty((NHEAD, 128, 2 * HC, 128), dtype=np.float32)
    wv_host = np.empty((NHEAD, 128, HC, 128), dtype=np.float32)
    for head in range(NHEAD):
        for src_role, dev_role in ((1, 0), (0, 1)):
            blk = wq4[:, src_role, head, :].reshape(HC, 128, 128).transpose(1, 0, 2)
            wqkv_host[head, :, dev_role * HC:(dev_role + 1) * HC, :] = blk
        wv_host[head] = wq4[:, 2, head, :].reshape(HC, 128, 128).transpose(1, 0, 2)
    wqkv_host = wqkv_host.reshape(NHEAD, 128, 2 * HC * 128)
    wv_host = wv_host.reshape(NHEAD, 128, HC * 128)

    wout_host = np.ascontiguousarray(
        wout_f.reshape(HC, 128, 16, 128).transpose(2, 1, 0, 3).reshape(16, 128, HC * 128))

    rwr = rw_f.reshape(O, E, H)
    rw_host = np.ascontiguousarray(
        rwr.reshape(NSPAN, SO, E, HC, 128).transpose(0, 4, 3, 2, 1).reshape(NSPAN, 128, HC * E * SO))
    rb_host = np.ascontiguousarray(
        rb_f.reshape(O, E).reshape(NSPAN, SO, E).transpose(0, 2, 1))

    ew_host = np.ascontiguousarray(
        ew_f.reshape(E, 16, 128, HC, 128).transpose(1, 4, 0, 3, 2).reshape(16, 128, E * HC * 128))

    kk = np.arange(128)[:, None]
    qq = np.arange(512)[None, :]
    tri_host = np.zeros((4, 128, 512), dtype=np.float32)
    for dd in range(4):
        tri_host[dd] = np.where(qq < kk + dd * 128, TRI_NEG, 0.0).astype(np.float32)
    tri_host = tri_host.astype(bf16)
    ident_host = np.eye(128, dtype=bf16)
    ue_host = np.zeros((E, E * 128), dtype=bf16)
    for e_ in range(E):
        ue_host[e_, e_ * 128:(e_ + 1) * 128] = 1.0

    in_maps = []
    for c in range(8):
        b_, j = c // 4, c % 4
        t1 = (j + 1) * D
        shift = S - t1
        xt_loc = np.zeros((H, S), dtype=np.float32)
        xt_loc[:, shift:] = x[b_, :t1, :].T
        xt_own_h = np.ascontiguousarray(
            x[b_, j * D:(j + 1) * D, :].T.reshape(16, 128, D))
        kb = np.empty((S,), dtype=np.float32)
        kb[:shift] = KB_NEG
        kb[shift:] = -CSHIFT
        in_maps.append({
            "xt": xt_loc,
            "xt_own": xt_own_h,
            "wqkv": wqkv_host,
            "wv": wv_host,
            "wout": wout_host,
            "rw": rw_host,
            "rb": rb_host,
            "ew": ew_host,
            "eb": eb_f,
            "tri": tri_host,
            "kbias": np.ascontiguousarray(kb.reshape(NKC, 128).T),
            "ident": ident_host,
            "ue": ue_host,
        })
    return in_maps


def kernel(L0_attn_x, L0_attn_w_qkv, L0_attn_w_out, L0_ffn_router_w,
           L0_ffn_router_b, L0_ffn_expert_w, L0_ffn_expert_b, _debug=False):
    in_maps = _host_layout(L0_attn_x, L0_attn_w_qkv, L0_attn_w_out,
                           L0_ffn_router_w, L0_ffn_router_b,
                           L0_ffn_expert_w, L0_ffn_expert_b)
    nc = build_program(debug=_debug)
    res = run_bass_kernel_spmd(nc, in_maps, core_ids=list(range(8)))
    out = np.empty((B, S, H), dtype=np.float32)
    for c in range(8):
        b_, j = c // 4, c % 4
        o = res.results[c]["out_t"]
        out[b_, j * D:(j + 1) * D, :] = o.transpose(2, 0, 1).reshape(D, H)
    if _debug:
        return out, res
    return out
